# version 01
# baseline (speedup 1.0000x reference)
import contextlib
import numpy as np
import ml_dtypes
import concourse.bass as bass
import concourse.mybir as mybir
from concourse.bass_utils import run_bass_kernel_spmd

F32 = mybir.dt.float32
BF16 = mybir.dt.bfloat16
ALU = mybir.AluOpType
AF = mybir.ActivationFunctionType
AX = mybir.AxisListType

T, D, INNER = 2048, 1024, 4096
NT, ND, NI = T // 128, D // 128, INNER // 128
C23 = 12582912.0
K52 = float(2 ** 52)
LN_EPS = 1e-5
ENGINES = ("sync", "vector", "scalar", "tensor", "gpsimd")


class P:
    def __init__(self, nc):
        self.nc = nc
        self.q = {e: [] for e in ENGINES}
        self.cnt = {n: 0 for n in ("V", "C", "T", "G", "dq", "gq")}
        self.sems = {}
        self.last_wait = {e: {} for e in ENGINES}

    def op(self, eng, fn, wait=(), inc=None, amt=1):
        waits = []
        for t in wait:
            if t is None:
                continue
            s, v = t
            if v > self.last_wait[eng].get(s, 0):
                waits.append((s, v))
                self.last_wait[eng][s] = v
        tok = None
        if inc:
            self.cnt[inc] += amt
            tok = (inc, self.cnt[inc])

        def closure(h, waits=waits, fn=fn, inc=inc, amt=amt):
            for s, v in waits:
                h.wait_ge(self.sems[s], v)
            inst = fn(h)
            if inc is not None:
                inst.then_inc(self.sems[inc], amt)

        self.q[eng].append(closure)
        return tok

    def v(self, fn, wait=()):    return self.op("vector", fn, wait, "V")
    def s(self, fn, wait=()):    return self.op("scalar", fn, wait, "C")
    def t(self, fn, wait=()):    return self.op("tensor", fn, wait, "T")
    def g(self, fn, wait=()):    return self.op("gpsimd", fn, wait, "G")
    def t_(self, fn, wait=()):   return self.op("tensor", fn, wait)
    def dma(self, fn, wait=()):  return self.op("sync", fn, wait, "dq", 16)
    def gdma(self, fn, wait=()): return self.op("gpsimd", fn, wait, "gq", 16)

    def finalize(self, out_tokens):
        nc = self.nc
        stack = contextlib.ExitStack()
        for name in self.cnt:
            self.sems[name] = stack.enter_context(nc.semaphore(name))

        def fin(h):
            for s, v in out_tokens:
                h.wait_ge(self.sems[s], v)

        self.q["sync"].append(fin)
        with nc.Block() as block:
            for reg, eng in ((block.sync, "sync"), (block.vector, "vector"),
                             (block.scalar, "scalar"), (block.tensor, "tensor"),
                             (block.gpsimd, "gpsimd")):
                closures = self.q[eng]

                def runner(h, closures=closures):
                    for c in closures:
                        c(h)

                reg(runner)
        stack.close()


class Arena:

    def __init__(self, tensor, total):
        self.t = tensor
        self.total = total
        self.off = 0

    def reset(self):
        self.off = 0

    def take(self, n):
        assert self.off + n <= self.total, f"arena overflow {self.off}+{n}>{self.total}"
        off = self.off
        self.off += n
        return self.t[:, off:off + n]


class MK:
    def __init__(self, sublayers=("S1", "S2", "S3", "S4", "S5", "S6")):
        self.sublayers = sublayers
        nc = bass.Bass("TRN2", target_bir_lowering=False)
        self.nc = nc
        self.I = {}
        need = set(sublayers)
        self.need = need

        def din(name, shape, dt=F32):
            self.I[name] = nc.dram_tensor(name, list(shape), dt, kind="ExternalInput").ap()

        din("x0", (T, D))
        if need & {"S4", "S6"}:
            din("w1", (D, INNER)); din("w2", (INNER, D))
            din("b1_row", (1, INNER)); din("b2_row", (1, D))
        if need & {"S1", "S3"}:
            din("WgT", (D, INNER)); din("WuT", (D, INNER)); din("WdT", (INNER, D))
        if "S2" in need:
            din("WwT", (D, 3 * D)); din("projT", (D, D))
            din("Wb_row", (1, 3 * D)); din("pjb_row", (1, D)); din("wb24_in", (128, 24))
            din("lng_rep", (128, D)); din("lnb_rep", (128, D))
            din("rpb_tiles", (16, 16, 128, 128))
        if "S5" in need:
            din("WkT", (D, D)); din("WvT", (D, D)); din("WrT", (D, D)); din("WoT", (D, D))
            din("bo_row", (1, D))
            din("u_in", (128, ND)); din("w_in", (128, ND))
            din("tmk_in", (128, ND)); din("tmv_in", (128, ND)); din("tmr_in", (128, ND))
        din("ones_col", (128, 1))
        din("ident_bf", (128, 128), BF16)
        din("ones_row_bf", (1, 128), BF16)

        self.out_ap = nc.dram_tensor("out", [T, D], F32, kind="ExternalOutput").ap()
        self.xres = nc.dram_tensor("xres", [T, D], F32).ap()
        if need & {"S1", "S3"}:
            self.tern_gu = nc.dram_tensor("tern_gu", [2 * D, INNER], BF16).ap()
            self.tern_d = nc.dram_tensor("tern_d", [INNER, D], BF16).ap()
            self.mw_scr = nc.dram_tensor("mw_scr", [3, 128], F32).ap()
            _hq = nc.dram_tensor("hq_scr", [NT, 128, INNER], BF16).ap()
            self.hq_scr = [_hq[c] for c in range(NT)]
        self.p = P(nc)
        self.A = {}
        self._consts()
        self.arb = Arena(nc.alloc_sbuf_tensor("ARB", [128, 70656], BF16), 70656)
        self.ar32 = Arena(nc.alloc_sbuf_tensor("AR32", [128, 8704], F32), 8704)
        self.prev_done = []

    def sb(self, name, shape, dt=F32):
        t = self.nc.alloc_sbuf_tensor(name, list(shape), dt)
        self.A[name] = t
        return t

    def psum(self, name, shape, dt=F32):
        t = self.nc.alloc_psum_tensor(name, list(shape), dt)
        self.A[name] = t
        return t

    def _consts(self):
        self.sb("onec", (128, 1)); self.sb("oracle", (128, 1)); self.sb("ceff", (128, 1))
        self.sb("ident", (128, 128), BF16)
        self.sb("ones_row", (1, 128), BF16)
        self.sb("mu", (128, NT)); self.sb("rstd", (128, NT))
        self.sb("stat", (128, 2 * NT))
        self.sb("sq_scrap", (128, 1024))
        self.sb("tiny", (128, 64))
        self.sb("cN49", (128, 1))
        self.sb("cP25", (128, 1))
        self.sb("bst", (1, 512))
        self.sb("bhi", (1, 4096), BF16)
        self.sb("blo", (1, 4096), BF16)
        PSA = self.nc.alloc_psum_tensor("PSA", [128, 1024], F32)
        PSB = self.nc.alloc_psum_tensor("PSB", [128, 1024], F32)
        PSC = self.nc.alloc_psum_tensor("PSC", [128, 1024], F32)
        self.A["PSA"], self.A["PSB"], self.A["PSC"] = PSA, PSB, PSC
        self.psum("ps_tp", (128, 1024), BF16)
        self.psum("PSD", (128, 512))
        self.A["psA0"] = PSA[:, 0:512]; self.A["psA1"] = PSA[:, 512:1024]
        self.A["psA2"] = PSB[:, 0:512]; self.A["psA3"] = PSB[:, 512:1024]
        self.A["psB0"] = PSC[:, 0:512]; self.A["psB1"] = PSC[:, 512:1024]

    def prologue(self):
        p, A, I = self.p, self.A, self.I
        t1 = p.dma(lambda h: h.dma_start(A["onec"][:], I["ones_col"]))
        self.tok_ident = p.dma(lambda h: h.dma_start(A["ident"][:], I["ident_bf"]))
        self.tok_onesrow = p.dma(lambda h: h.dma_start(A["ones_row"][:], I["ones_row_bf"]))
        p.op("gpsimd", lambda h: h.memset(A["cN49"][:], -0.4921875))
        self.tok_c49 = p.g(lambda h: h.memset(A["cP25"][:], 0.25))
        tk = p.g(lambda h: h.tensor_scalar(A["oracle"][:], A["onec"][:], K52, K52, ALU.add, ALU.subtract), wait=[t1])
        self.tok_ceff = p.g(lambda h: h.tensor_scalar(A["ceff"][:], A["oracle"][:], K52, C23, ALU.mult, ALU.add), wait=[tk])

    def bias_hilo(self, src_row, n, off=0, scale=1.0):
        return self.bias_hilo_to(src_row, n, self.A["bhi"], self.A["blo"], off=off, scale=scale)

    def bias_hilo_to(self, src_row, n, hi_t, lo_t, off=0, scale=1.0, extra_wait=()):
        p, A = self.p, self.A
        tok = None
        prevd = None
        for k in range(0, n, 512):
            w = min(512, n - k)
            t0 = p.dma(lambda h, k=k, w=w: h.dma_start(A["bst"][:1, :w], src_row[:1, k:k + w]),
                       wait=list(self.prev_done) + list(extra_wait) + [prevd, tok])
            if scale != 1.0:
                t0 = p.v(lambda h, w=w, scale=scale: h.tensor_single_scalar(A["bst"][:1, :w], A["bst"][:1, :w], scale, ALU.mult), wait=[t0])
            t1 = p.v(lambda h, k=k, w=w, hi_t=hi_t: h.tensor_copy(hi_t[:1, off + k:off + k + w], A["bst"][:1, :w]), wait=[t0])
            tok = p.v(lambda h, k=k, w=w, hi_t=hi_t, lo_t=lo_t: h.tensor_tensor(
                lo_t[:1, off + k:off + k + w], A["bst"][:1, :w], hi_t[:1, off + k:off + k + w], ALU.subtract))
            prevd = t0
        return tok

    def ln_stats(self, c, xc_ap, wait):
        p, A = self.p, self.A
        t1 = p.v(lambda h: h.reduce_sum(A["stat"][:, c:c + 1], xc_ap, AX.X), wait=wait)
        t2 = p.s(lambda h: h.activation(A["sq_scrap"][:], xc_ap, AF.Square,
                                        accum_out=A["stat"][:, NT + c:NT + c + 1]), wait=wait)
        u = (c % 4) * 4
        ty = A["tiny"]
        t3 = p.g(lambda h, c=c: h.tensor_single_scalar(A["mu"][:, c:c + 1], A["stat"][:, c:c + 1], 1.0 / D, ALU.mult), wait=[t1])
        t4 = p.g(lambda h, c=c, u=u: h.tensor_tensor(ty[:, u:u + 1], A["mu"][:, c:c + 1], A["mu"][:, c:c + 1], ALU.mult), wait=[t3])
        t5 = p.g(lambda h, c=c, u=u: h.tensor_scalar(ty[:, u + 1:u + 2], A["stat"][:, NT + c:NT + c + 1], 1.0 / D, LN_EPS, ALU.mult, ALU.add), wait=[t2])
        t6 = p.g(lambda h, u=u: h.tensor_tensor(ty[:, u + 2:u + 3], ty[:, u + 1:u + 2], ty[:, u:u + 1], ALU.subtract), wait=[t4, t5])
        t7 = p.v(lambda h, u=u: h.reciprocal(ty[:, u + 3:u + 4], ty[:, u + 2:u + 3]), wait=[t6])
        t8 = p.s(lambda h, c=c, u=u: h.activation(A["rstd"][:, c:c + 1], ty[:, u + 3:u + 4], AF.Sqrt), wait=[t7])
        return t8

    def emit_l1ffn(self, src, dst):
        p, A, I = self.p, self.A, self.I
        self.arb.reset(); self.ar32.reset()
        arb, ar32 = self.arb, self.ar32
        w1b = arb.take(ND * INNER)
        w2b = arb.take(NI * D)
        xlnw = arb.take(D)
        xlnT = xlnw
        hq = arb.take(INNER)
        hqT = hq
        xw = ar32.take(2 * D)
        xnew = ar32.take(D)
        pw_u = ar32.take(512); pw_z = ar32.take(512); pw_f = ar32.take(512)
        pw_a = ar32.take(2 * 512); pw_j = ar32.take(2 * 512)
        pw_g1 = ar32.take(2 * 512); pw_g2 = ar32.take(2 * 512)
        pw_e = pw_a

        prev = list(self.prev_done)
        wtoks = []
        for k in range(ND):
            wtoks.append(p.gdma(lambda h, k=k: h.dma_start(w1b[:, k * INNER:(k + 1) * INNER],
                                                           I["w1"][k * 128:(k + 1) * 128, :]), wait=prev))
        for k in range(NI):
            wtoks.append(p.gdma(lambda h, k=k: h.dma_start(w2b[:, k * D:(k + 1) * D],
                                                           I["w2"][k * 128:(k + 1) * 128, :]), wait=prev))
        tok_w1 = wtoks[ND - 1]
        tok_w2 = wtoks[-1]
        if "b2hi" not in self.A:
            self.sb("b2hi", (1, D), BF16); self.sb("b2lo", (1, D), BF16)
        b1hi_ap = A["bhi"][:1, 0:INNER]; b1lo_ap = A["blo"][:1, 0:INNER]
        tok_b1 = self.bias_hilo(I["b1_row"], INNER, off=0, scale=1.0)
        tok_b2 = self.bias_hilo_to(I["b2_row"], D, A["b2hi"], A["b2lo"], scale=256.0, extra_wait=[tok_b1])

        psH = [A["psA0"], A["psA1"], A["psA2"], A["psA3"]]
        psO = [A["psB0"], A["psB1"]]
        tok_psH = [None] * 4
        tok_psO = [None, None]
        tok_xload = [None] * NT
        tok_xfree = [None] * NT
        tok_lnT_evac = [None] * NT
        tok_lnT_free = None
        tok_hq_free = None
        tok_hqT_free = None
        tok_xn_dma = [None] * NT
        tok_xn_free = [None] * NT
        tok_tp = None
        tok_xlnw_free = [None] * NT
        tok_a = [None, None]; tok_j_s = [None, None]; tok_j_v = [None, None]
        tok_g1 = [None, None]; tok_g2 = [None, None]; tok_e = [None, None]
        mg = 0
        tV12 = None

        for c in range(NT):
            xsl = xw[:, (c % 2) * D:((c % 2) + 1) * D]
            tok_xload[c] = p.dma(lambda h, c=c, xsl=xsl: h.dma_start(xsl, src[c * 128:(c + 1) * 128, :]),
                                 wait=[tok_xfree[c - 2] if c >= 2 else (prev[-1] if prev else None)])
            tstat = self.ln_stats(c, xsl, [tok_xload[c]])
            lnsl = xlnw[:, 0:D]
            tok_ln = p.v(lambda h, xsl=xsl, lnsl=lnsl, c=c: h.tensor_scalar(
                lnsl, xsl, A["mu"][:, c:c + 1], A["rstd"][:, c:c + 1], ALU.subtract, ALU.mult),
                wait=[tstat, tok_xlnw_free[c - 1] if c >= 1 else None, tok_lnT_free])
            tps = []
            for j in range(ND):
                w = [tok_ln, self.tok_ident, tok_tp] if j == 0 else []
                tps.append(p.t(lambda h, j=j, lnsl=lnsl: h.transpose(
                    A["ps_tp"][:, j * 128:(j + 1) * 128], lnsl[:, j * 128:(j + 1) * 128], A["ident"][:]), wait=w))
            tok_xlnw_free[c] = tps[-1]
            dst3 = xlnT.rearrange("p (j n) -> p j n", j=ND)
            src3 = A["ps_tp"][:].rearrange("p (j n) -> p j n", j=ND)
            tok_lnT_evac[c] = p.v(lambda h, dst3=dst3, src3=src3: h.tensor_copy(dst3, src3),
                                  wait=[tps[-1], tok_lnT_free])
            tok_tp = tok_lnT_evac[c]

            last_mm1 = None
            for q in range(8):
                bank = mg % 4
                qpar = mg % 2
                for k in range(ND):
                    lhsT = xlnT[:, k * 128:(k + 1) * 128]
                    rhs = w1b[:, k * INNER + q * 512: k * INNER + (q + 1) * 512]
                    w = [tok_lnT_evac[c], tok_w1, tok_psH[bank], tok_b1, self.tok_onesrow] if k == 0 else []
                    p.t_(lambda h, lhsT=lhsT, rhs=rhs, k=k, bank=bank: h.matmul(
                        psH[bank][:], lhsT, rhs, start=(k == 0), stop=False), wait=w)
                p.t_(lambda h, bank=bank, q=q: h.matmul(
                    psH[bank][:], A["ones_row"][:1, :], b1hi_ap[:, q * 512:(q + 1) * 512], start=False, stop=False))
                last_mm1 = p.t(lambda h, bank=bank, q=q: h.matmul(
                    psH[bank][:], A["ones_row"][:1, :], b1lo_ap[:, q * 512:(q + 1) * 512], start=False, stop=True))
                tV1 = p.v(lambda h, bank=bank: h.tensor_scalar(
                    pw_u, psH[bank][:], 256.0, A["ceff"][:, :1], ALU.mult, ALU.add),
                    wait=[last_mm1, self.tok_ceff])
                tok_psH[bank] = tV1
                tV2 = p.v(lambda h: h.tensor_single_scalar(pw_z, pw_u, A["ceff"][:, :1], ALU.subtract))
                asl = pw_a[:, qpar * 512:(qpar + 1) * 512]
                tA3 = p.s(lambda h, asl=asl: h.activation(asl, pw_z, AF.Copy, bias=-0.4921875, scale=1.0 / 64),
                          wait=[tV2, tok_a[qpar], self.tok_c49])
                jsl = pw_j[:, qpar * 512:(qpar + 1) * 512]
                tG4 = p.g(lambda h, asl=asl, jsl=jsl: h.tensor_scalar(
                    jsl, asl, A["ceff"][:, :1], A["ceff"][:, :1], ALU.add, ALU.subtract),
                    wait=[tA3, tok_j_s[qpar], tok_j_v[qpar], self.tok_ceff])
                g1sl = pw_g1[:, qpar * 512:(qpar + 1) * 512]
                g2sl = pw_g2[:, qpar * 512:(qpar + 1) * 512]
                tS5 = p.s(lambda h, jsl=jsl, g1sl=g1sl: h.activation(g1sl, jsl, AF.Gelu, scale=0.25),
                          wait=[tG4, tok_g1[qpar]])
                tS6 = p.s(lambda h, jsl=jsl, g2sl=g2sl: h.activation(g2sl, jsl, AF.Gelu, scale=0.25, bias=A["cP25"][:, :1]),
                          wait=[tok_g2[qpar]])
                tok_j_s[qpar] = tS6
                tV8 = p.v(lambda h, jsl=jsl: h.scalar_tensor_tensor(
                    pw_f, jsl, -64.0, pw_z, ALU.mult, ALU.add), wait=[tG4])
                tok_j_v[qpar] = tV8
                esl = pw_e[:, qpar * 512:(qpar + 1) * 512]
                tG7 = p.g(lambda h, esl=esl, g1sl=g1sl, g2sl=g2sl: h.tensor_tensor(esl, g2sl, g1sl, ALU.subtract),
                          wait=[tS6, tS5, tok_e[qpar]])
                tok_g2[qpar] = tG7
                tV9 = p.v(lambda h, esl=esl: h.scalar_tensor_tensor(
                    pw_z, pw_f, 1.0 / 64, esl, ALU.mult, ALU.mult), wait=[tG7])
                tok_e[qpar] = tV9
                tok_a[qpar] = tV9
                tV10 = p.v(lambda h, g1sl=g1sl: h.tensor_tensor(pw_f, pw_z, g1sl, ALU.add))
                tok_g1[qpar] = tV10
                tV11 = p.v(lambda h: h.tensor_scalar(pw_u, pw_f, 256.0, A["ceff"][:, :1], ALU.mult, ALU.add))
                hqsl = hq[:, q * 512:(q + 1) * 512]
                tV12 = p.v(lambda h, hqsl=hqsl: h.tensor_single_scalar(hqsl, pw_u, A["ceff"][:, :1], ALU.subtract),
                           wait=[tok_hq_free, tok_hqT_free])
                mg += 1
            tok_lnT_free = last_mm1

            last_tp = None
            for r in range(4):
                for jj in range(8):
                    j = r * 8 + jj
                    src_sl = hq[:, j * 128:(j + 1) * 128]
                    w = [tV12, tok_tp] if jj == 0 else []
                    last_tp = p.t(lambda h, jj=jj, src_sl=src_sl: h.transpose(
                        A["ps_tp"][:, jj * 128:(jj + 1) * 128], src_sl, A["ident"][:]), wait=w)
                dst3 = hqT.rearrange("p (r j n) -> p r j n", r=4, j=8)[:, r]
                src3 = A["ps_tp"][:].rearrange("p (j n) -> p j n", j=8)
                ev = p.v(lambda h, dst3=dst3, src3=src3: h.tensor_copy(dst3, src3),
                         wait=[last_tp, tok_hqT_free if r == 0 else None])
                tok_tp = ev
            tok_hq_free = last_tp
            tok_hqT_evac = tok_tp

            xnsl = xnew[:, 0:D]
            tok_o = [None, None]
            for dh in range(2):
                for k in range(NI):
                    lhsT = hqT[:, k * 128:(k + 1) * 128]
                    rhs = w2b[:, k * D + dh * 512: k * D + (dh + 1) * 512]
                    w = [tok_hqT_evac, tok_w2, tok_psO[dh], tok_b2] if k == 0 else []
                    p.t_(lambda h, lhsT=lhsT, rhs=rhs, k=k, dh=dh: h.matmul(
                        psO[dh][:], lhsT, rhs, start=(k == 0), stop=False), wait=w)
                p.t_(lambda h, dh=dh: h.matmul(
                    psO[dh][:], A["ones_row"][:1, :], A["b2hi"][:1, dh * 512:(dh + 1) * 512], start=False, stop=False))
                tok_o[dh] = p.t(lambda h, dh=dh: h.matmul(
                    psO[dh][:], A["ones_row"][:1, :], A["b2lo"][:1, dh * 512:(dh + 1) * 512], start=False, stop=True))
            evs = [None, None]
            for dh in range(2):
                tE1 = p.v(lambda h, dh=dh: h.tensor_scalar(
                    pw_u, psO[dh][:], 1.0, A["ceff"][:, :1], ALU.mult, ALU.add), wait=[tok_o[dh], self.tok_ceff])
                tok_psO[dh] = tE1
                tE2 = p.v(lambda h: h.tensor_single_scalar(pw_z, pw_u, A["ceff"][:, :1], ALU.subtract))
                evs[dh] = p.v(lambda h, xsl=xsl, dh=dh, xnsl=xnsl: h.scalar_tensor_tensor(
                    xnsl[:, dh * 512:(dh + 1) * 512], pw_z, 1.0 / 512,
                    xsl[:, dh * 512:(dh + 1) * 512], ALU.mult, ALU.add),
                    wait=[tok_xn_free[c - 1] if c >= 1 else None])
            tok_hqT_free = tok_o[1]
            tok_xfree[c] = evs[1]
            tok_xn_dma[c] = p.dma(lambda h, c=c, xnsl=xnsl: h.dma_start(dst[c * 128:(c + 1) * 128, :], xnsl),
                                  wait=[evs[1]])
            tok_xn_free[c] = tok_xn_dma[c]

        self.prev_done = [tok_xn_dma[c] for c in range(NT)] + [tV12, tok_hq_free]
        return [tok_xn_dma[c] for c in range(NT)]


    def emit_l1attn(self, src, dst):
        p, A, I = self.p, self.A, self.I
        self.arb.reset(); self.ar32.reset()
        arb, ar32 = self.arb, self.ar32
        SC_T = 1024
        wk = arb.take(ND * D)
        wv = arb.take(ND * D)
        wr = arb.take(ND * D)
        wo = arb.take(ND * D)
        xlnT = arb.take(ND * SC_T)
        mixk = arb.take(ND * SC_T)
        mixv = arb.take(ND * SC_T)
        mixr = arb.take(ND * SC_T)
        xlncw = arb.take(D)
        prevcol = arb.take(ND)
        bufA = ar32.take(SC_T); bufB = ar32.take(SC_T); bufC = ar32.take(SC_T)
        bufD = ar32.take(SC_T); bufE = ar32.take(SC_T)
        xw = ar32.take(2 * D)
        xnew = ar32.take(D)
        st8 = ar32.take(32)
        prev = list(self.prev_done)

        wtoks = []
        for name, buf in (("WkT", wk), ("WvT", wv), ("WrT", wr), ("WoT", wo)):
            for k in range(ND):
                wtoks.append(p.gdma(lambda h, k=k, buf=buf, name=name: h.dma_start(
                    buf[:, k * D:(k + 1) * D], I[name][k * 128:(k + 1) * 128, :]), wait=prev))
        tok_wk, tok_wv, tok_wr, tok_wo = wtoks[ND - 1], wtoks[2 * ND - 1], wtoks[3 * ND - 1], wtoks[4 * ND - 1]
        if "u8" not in A:
            self.sb("u8", (128, ND)); self.sb("w8", (128, ND)); self.sb("nw8", (128, ND))
            self.sb("tk8", (128, ND)); self.sb("tv8", (128, ND)); self.sb("tr8", (128, ND))
            self.sb("ok8", (128, ND)); self.sb("ov8", (128, ND)); self.sb("or8", (128, ND))
        ctoks = []
        for name, dstn in (("u_in", "u8"), ("w_in", "w8"), ("tmk_in", "tk8"),
                           ("tmv_in", "tv8"), ("tmr_in", "tr8")):
            ctoks.append(p.dma(lambda h, name=name, dstn=dstn: h.dma_start(A[dstn][:], I[name]), wait=prev))
        tok_c = ctoks[-1]
        t1 = p.g(lambda h: h.tensor_single_scalar(A["nw8"][:], A["w8"][:], -1.0, ALU.mult), wait=ctoks)
        t2 = p.g(lambda h: h.tensor_scalar(A["ok8"][:], A["tk8"][:], -1.0, 1.0, ALU.mult, ALU.add))
        t3 = p.g(lambda h: h.tensor_scalar(A["ov8"][:], A["tv8"][:], -1.0, 1.0, ALU.mult, ALU.add))
        t4 = p.g(lambda h: h.tensor_scalar(A["or8"][:], A["tr8"][:], -1.0, 1.0, ALU.mult, ALU.add))
        tok_vec = p.g(lambda h: h.memset(st8[:, 0:16], 0.0), wait=[t1, t2, t3, t4])
        tok_pc = p.g(lambda h: h.memset(prevcol, 0.0))
        tok_bo = self.bias_hilo_to(I["bo_row"], D, A["b2hi"] if "b2hi" in A else self.sb("b2hi", (1, D), BF16),
                                   A["b2lo"] if "b2lo" in A else self.sb("b2lo", (1, D), BF16))
        bo_hi, bo_lo = A["b2hi"], A["b2lo"]

        psA = [A["psA0"], A["psA1"], A["psA2"], A["psA3"]]
        psB = [A["psB0"], A["psB1"]]
        tok_psA = [None] * 4
        tok_psB = [None, None]
        tok_tp = None
        tok_xload = [None] * NT
        tok_xfree = [None] * NT
        tok_xn_dma = [None] * NT
        tok_lnT_evac = [None] * NT
        tok_mix_done = None
        tok_y = [None] * ND
        tok_prevcol_saved = None
        tok_state = [None] * ND
        tok_xlncw_free = None

        R3 = lambda ap, j: ap

        for SC in range(2):
            for cl in range(8):
                gc = SC * 8 + cl
                xsl = xw[:, (gc % 2) * D:((gc % 2) + 1) * D]
                tok_xload[gc] = p.dma(lambda h, gc=gc, xsl=xsl: h.dma_start(xsl, src[gc * 128:(gc + 1) * 128, :]),
                                      wait=[tok_xfree[gc - 2] if gc >= 2 else (prev[-1] if prev else None)])
                tstat = self.ln_stats(gc, xsl, [tok_xload[gc]])
                tok_ln = p.v(lambda h, xsl=xsl, gc=gc: h.tensor_scalar(
                    xlncw, xsl, A["mu"][:, gc:gc + 1], A["rstd"][:, gc:gc + 1], ALU.subtract, ALU.mult),
                    wait=[tstat, tok_xlncw_free])
                tok_xfree[gc] = tok_ln
                tps = []
                for j in range(ND):
                    w = [tok_ln, self.tok_ident, tok_tp, tok_y[ND - 1]] if j == 0 else []
                    tps.append(p.t(lambda h, j=j: h.transpose(
                        A["ps_tp"][:, j * 128:(j + 1) * 128], xlncw[:, j * 128:(j + 1) * 128], A["ident"][:]), wait=w))
                tok_xlncw_free = tps[-1]
                dst3 = xlnT.rearrange("p (j n) -> p j n", j=ND)[:, :, cl * 128:(cl + 1) * 128]
                src3 = A["ps_tp"][:].rearrange("p (j n) -> p j n", j=ND)
                tok_lnT_evac[gc] = p.v(lambda h, dst3=dst3, src3=src3: h.tensor_copy(dst3, src3), wait=[tps[-1]])
                tok_tp = tok_lnT_evac[gc]
            tok_phaseA = tok_lnT_evac[SC * 8 + 7]

            mix_toks = []
            for (mx, t8, o8, tokw) in ((mixk, "tk8", "ok8", tok_vec), (mixv, "tv8", "ov8", tok_vec), (mixr, "tr8", "or8", tok_vec)):
                for j in range(ND):
                    xj = xlnT[:, j * SC_T:(j + 1) * SC_T]
                    mj = mx[:, j * SC_T:(j + 1) * SC_T]
                    ta = p.v(lambda h, xj=xj, mj=mj, t8=t8, j=j: h.tensor_single_scalar(
                        mj, xj, A[t8][:, j:j + 1], ALU.mult), wait=[tok_phaseA, tokw, tok_mix_done])
                    tb = p.v(lambda h, xj=xj, mj=mj, o8=o8, j=j: h.scalar_tensor_tensor(
                        mj[:, 1:SC_T], xj[:, 0:SC_T - 1], A[o8][:, j:j + 1], mj[:, 1:SC_T], ALU.mult, ALU.add))
                    tc = p.v(lambda h, mj=mj, o8=o8, j=j: h.scalar_tensor_tensor(
                        mj[:, 0:1], prevcol[:, j:j + 1], A[o8][:, j:j + 1], mj[:, 0:1], ALU.mult, ALU.add),
                        wait=[tok_pc])
                    mix_toks.append(tc)
            sig_toks = []
            for j in range(ND):
                mj = mixr[:, j * SC_T:(j + 1) * SC_T]
                sig_toks.append(p.s(lambda h, mj=mj: h.activation(mj, mj, AF.Sigmoid), wait=[mix_toks[-1]]))
            srcc = xlnT.rearrange("p (j n) -> p j n", j=ND)[:, :, SC_T - 1:SC_T]
            tok_pc = p.v(lambda h, srcc=srcc: h.tensor_copy(
                prevcol.rearrange("p (j one) -> p j one", one=1), srcc), wait=[tok_phaseA])
            tok_mix = mix_toks[-1]
            tok_sig = sig_toks[-1]

            tok_y = [None] * ND
            for e in range(ND):
                mm_k = mm_v = None
                for n in range(2):
                    for kt in range(ND):
                        lhsT = wk[:, kt * D + e * 128: kt * D + e * 128 + 128]
                        rhs = mixk[:, kt * SC_T + n * 512: kt * SC_T + (n + 1) * 512]
                        w = [tok_mix, tok_wk, tok_psA[n]] if kt == 0 else []
                        if kt < ND - 1:
                            p.t_(lambda h, lhsT=lhsT, rhs=rhs, kt=kt, n=n: h.matmul(
                                psA[n][:], lhsT, rhs, start=(kt == 0), stop=False), wait=w)
                        else:
                            mm_k = p.t(lambda h, lhsT=lhsT, rhs=rhs, n=n: h.matmul(
                                psA[n][:], lhsT, rhs, start=False, stop=True))
                    tok_psA[n] = mm_k
                for n in range(2):
                    for kt in range(ND):
                        lhsT = wv[:, kt * D + e * 128: kt * D + e * 128 + 128]
                        rhs = mixv[:, kt * SC_T + n * 512: kt * SC_T + (n + 1) * 512]
                        w = [tok_mix, tok_wv, tok_psA[2 + n]] if kt == 0 else []
                        if kt < ND - 1:
                            p.t_(lambda h, lhsT=lhsT, rhs=rhs, kt=kt, n=n: h.matmul(
                                psA[2 + n][:], lhsT, rhs, start=(kt == 0), stop=False), wait=w)
                        else:
                            mm_v = p.t(lambda h, lhsT=lhsT, rhs=rhs, n=n: h.matmul(
                                psA[2 + n][:], lhsT, rhs, start=False, stop=True))
                    tok_psA[2 + n] = mm_v
                ek1 = p.v(lambda h: h.tensor_copy(bufA[:, 0:512], psA[0][:]), wait=[mm_k])
                ek2 = p.v(lambda h: h.tensor_copy(bufA[:, 512:1024], psA[1][:]))
                ev1 = p.v(lambda h: h.tensor_copy(bufB[:, 0:512], psA[2][:]), wait=[mm_v])
                ev2 = p.v(lambda h: h.tensor_copy(bufB[:, 512:1024], psA[3][:]))
                tok_psA = [ek2, ek2, ev2, ev2]
                mm_r = None
                for n in range(2):
                    for kt in range(ND):
                        lhsT = wr[:, kt * D + e * 128: kt * D + e * 128 + 128]
                        rhs = mixr[:, kt * SC_T + n * 512: kt * SC_T + (n + 1) * 512]
                        w = [tok_sig, tok_wr, tok_psB[n]] if kt == 0 else []
                        if kt < ND - 1:
                            p.t_(lambda h, lhsT=lhsT, rhs=rhs, kt=kt, n=n: h.matmul(
                                psB[n][:], lhsT, rhs, start=(kt == 0), stop=False), wait=w)
                        else:
                            mm_r = p.t(lambda h, lhsT=lhsT, rhs=rhs, n=n: h.matmul(
                                psB[n][:], lhsT, rhs, start=False, stop=True))
                    tok_psB[n] = mm_r
                er1 = p.v(lambda h: h.tensor_copy(bufC[:, 0:512], psB[0][:]), wait=[mm_r])
                er2 = p.v(lambda h: h.tensor_copy(bufC[:, 512:1024], psB[1][:]))
                tok_psB = [er2, er2]
                tq = p.v(lambda h, e=e: h.tensor_scalar(bufD, bufA, A["u8"][:, e:e + 1], A["w8"][:, e:e + 1],
                                                        ALU.add, ALU.max), wait=[ek2, tok_c, tok_vec])
                te1 = p.s(lambda h, e=e: h.activation(bufE, bufD, AF.Exp, bias=A["nw8"][:, e:e + 1], scale=-1.0),
                          wait=[tq])
                tkq = p.v(lambda h: h.scalar_tensor_tensor(bufA, bufD, -1.0, bufA, ALU.mult, ALU.add), wait=[tq])
                te2 = p.s(lambda h, e=e: h.activation(bufD, bufA, AF.Exp, bias=A["u8"][:, e:e + 1]), wait=[tkq, te1])
                te2v = p.v(lambda h: h.tensor_mul(bufB, bufD, bufB), wait=[te2, ev2])
                tsa = p.v(lambda h, e=e: h.tensor_tensor_scan(bufA, bufE, bufB, st8[:, e:e + 1], ALU.mult, ALU.add),
                          wait=[te2v, te1, tok_state[e]])
                if SC == 0 and e == 7 and getattr(self, "dbg_s5", False):
                    p.v(lambda h: h.tensor_copy(A["sq_scrap"][:, 40:48], bufA[:, SC_T - 8:SC_T]))
                    p.v(lambda h: h.tensor_copy(A["sq_scrap"][:, 48:56], bufE[:, SC_T - 8:SC_T]))
                    p.v(lambda h: h.tensor_copy(A["sq_scrap"][:, 56:64], bufB[:, SC_T - 8:SC_T]))
                tst1 = p.v(lambda h, e=e: h.tensor_copy(st8[:, e:e + 1], bufA[:, SC_T - 1:SC_T]), wait=[tsa])
                tsb = p.v(lambda h, e=e: h.tensor_tensor_scan(bufB, bufE, bufD, st8[:, 8 + e:9 + e], ALU.mult, ALU.add))
                tst2 = p.v(lambda h, e=e: h.tensor_copy(st8[:, 8 + e:9 + e], bufB[:, SC_T - 1:SC_T]), wait=[tsb])
                tok_state[e] = tst2
                trb = p.v(lambda h: h.reciprocal(bufE, bufB), wait=[tsb])
                tm = p.v(lambda h: h.tensor_mul(bufA, bufA, bufE), wait=[tsa])
                ysl = xlnT[:, e * SC_T:(e + 1) * SC_T]
                tok_y[e] = p.v(lambda h, ysl=ysl: h.tensor_mul(ysl, bufA, bufC),
                               wait=[er2, tok_pc, tok_prevcol_saved])
            tok_mix_done = tok_y[ND - 1]

            for cl in range(8):
                gc = SC * 8 + cl
                xsl = xw[:, (gc % 2) * D:((gc % 2) + 1) * D]
                tok_x2 = p.dma(lambda h, gc=gc, xsl=xsl: h.dma_start(xsl, src[gc * 128:(gc + 1) * 128, :]),
                               wait=[tok_xfree[SC * 8 + 6 + (gc % 2)], tok_xn_dma[gc - 2] if gc >= 2 else None])
                mm_o = [None, None]
                for n in range(2):
                    for e in range(ND):
                        lhsT = xlnT[:, e * SC_T + cl * 128: e * SC_T + cl * 128 + 128]
                        rhs = wo[:, e * D + n * 512: e * D + (n + 1) * 512]
                        w = [tok_y[ND - 1], tok_wo, tok_psB[n], tok_bo, self.tok_onesrow] if e == 0 else []
                        p.t_(lambda h, lhsT=lhsT, rhs=rhs, e=e, n=n: h.matmul(
                            psB[n][:], lhsT, rhs, start=(e == 0), stop=False), wait=w)
                    p.t_(lambda h, n=n: h.matmul(psB[n][:], A["ones_row"][:1, :],
                                                 bo_hi[:1, n * 512:(n + 1) * 512], start=False, stop=False))
                    mm_o[n] = p.t(lambda h, n=n: h.matmul(psB[n][:], A["ones_row"][:1, :],
                                                          bo_lo[:1, n * 512:(n + 1) * 512], start=False, stop=True))
                evs = [None, None]
                for n in range(2):
                    evs[n] = p.v(lambda h, n=n, xsl=xsl: h.tensor_add(
                        xnew[:, n * 512:(n + 1) * 512], psB[n][:], xsl[:, n * 512:(n + 1) * 512]),
                        wait=[mm_o[n], tok_x2, tok_xn_dma[gc - 1] if gc >= 1 else None])
                    tok_psB[n] = evs[n]
                tok_xfree[gc] = evs[1]
                tok_xn_dma[gc] = p.dma(lambda h, gc=gc: h.dma_start(dst[gc * 128:(gc + 1) * 128, :], xnew[:, 0:D]),
                                       wait=[evs[1]])
            tok_prevcol_saved = tok_pc
            if SC == 0 and getattr(self, "dbg_s5", False):
                dbg_st = self.nc.dram_tensor("dbg_st", [128, 32], F32, kind="ExternalOutput").ap()
                dbg_pc = self.nc.dram_tensor("dbg_pc", [128, ND], F32, kind="ExternalOutput").ap()
                ttmp = p.v(lambda h: h.tensor_copy(A["sq_scrap"][:, 0:ND], prevcol), wait=[tok_pc, tok_state[ND-1]])
                tcap = p.v(lambda h: h.tensor_copy(A["sq_scrap"][:, 8:40], st8[:, 0:32]))
                dbg_sc = self.nc.dram_tensor("dbg_sc", [128, 24], F32, kind="ExternalOutput").ap()
                self.dbg_toks = [
                    p.dma(lambda h: h.dma_start(dbg_st, A["sq_scrap"][:, 8:40]), wait=[tcap]),
                    p.dma(lambda h: h.dma_start(dbg_pc, A["sq_scrap"][:, 0:ND]), wait=[ttmp]),
                    p.dma(lambda h: h.dma_start(dbg_sc, A["sq_scrap"][:, 40:64]), wait=[tcap]),
                ]
            if SC == 1 and getattr(self, "dbg_s5", False):
                dbg_mk = self.nc.dram_tensor("dbg_mk", [128, 8 * ND], F32, kind="ExternalOutput").ap()
                dbg_k0 = self.nc.dram_tensor("dbg_k0", [128, 8], F32, kind="ExternalOutput").ap()
                src3 = mixk.rearrange("p (j n) -> p j n", j=ND)[:, :, 0:8]
                ttmp2 = p.v(lambda h, src3=src3: h.tensor_copy(A["sq_scrap"][:, 0:8 * ND].rearrange("p (j n) -> p j n", j=ND), src3),
                            wait=[tok_mix, tok_y[ND-1]])
                self.dbg_toks += [p.dma(lambda h: h.dma_start(dbg_mk, A["sq_scrap"][:, 0:8 * ND]), wait=[ttmp2])]
                ttmp3 = p.v(lambda h: h.tensor_copy(A["sq_scrap"][:, 32:40], A["sq_scrap"][:, 32:40]))
                self.dbg_toks += [p.dma(lambda h: h.dma_start(dbg_k0, A["sq_scrap"][:, 32:40]), wait=[ttmp3])]

        self.prev_done = [tok_xn_dma[c] for c in range(NT)] + [tok_y[ND - 1]]
        return [tok_xn_dma[c] for c in range(NT)] + (getattr(self, "dbg_toks", []) if getattr(self, "dbg_s5", False) else [])



    def quant_matrix(self, iname, ntiles, C, dst_of, spill, mw_col, total_elems, prev):
        p, A, I = self.p, self.A, self.I
        wld = self.q_wld
        qs = A["qs"]
        toks_ld = [None] * ntiles
        toks_sum = [None] * ntiles
        for r in range(ntiles):
            sl = wld[:, (r % 2) * C:((r % 2) + 1) * C]
            toks_ld[r] = p.dma(lambda h, r=r, sl=sl: h.dma_start(sl, I[iname][r * 128:(r + 1) * 128, :]),
                               wait=list(prev) + [toks_sum[r - 2] if r >= 2 else None])
            toks_sum[r] = p.v(lambda h, r=r, sl=sl: h.reduce_sum(qs[:, r:r + 1], sl, AX.X,
                                                                 apply_absolute_value=True), wait=[toks_ld[r]])
        tcs = p.v(lambda h, ntiles=ntiles: h.reduce_sum(qs[:, 32:33], qs[:, 0:ntiles], AX.X), wait=[toks_sum[-1]])
        tmm = p.t(lambda h: h.matmul(A["psB0"][:1, 0:1], A["onecf"][:], qs[:, 32:33],
                                     start=True, stop=True), wait=[tcs, self.tok_onecf, self.tok_psB0_q])
        tev = p.v(lambda h: h.tensor_copy(A["tiny"][:1, 60:61], A["psB0"][:1, 0:1]), wait=[tmm])
        self.tok_psB0_q = tev
        tg1 = p.g(lambda h, total_elems=total_elems: h.tensor_scalar(
            A["tiny"][:1, 61:62], A["tiny"][:1, 60:61], 1.0 / total_elems, 1e-5, ALU.mult, ALU.max), wait=[tev])
        tbc = p.t(lambda h: h.matmul(A["psB1"][:, 0:1], A["onesrf"][:1, :], A["tiny"][:1, 61:62],
                                     start=True, stop=True), wait=[tg1, self.tok_onesrf, self.tok_psB1_q])
        tmw = p.v(lambda h, mw_col=mw_col: h.tensor_copy(mw_col, A["psB1"][:, 0:1]), wait=[tbc])
        self.tok_psB1_q = tmw
        tthr = p.g(lambda h, mw_col=mw_col: h.tensor_single_scalar(A["thrq"][:, 0:1], mw_col, 0.5, ALU.mult),
                   wait=[tmw])
        tnthr = p.g(lambda h, mw_col=mw_col: h.tensor_single_scalar(A["thrq"][:, 1:2], mw_col, -0.5, ALU.mult))
        toks_q = [None] * ntiles
        tok_rd = [None] * ntiles
        for r in range(ntiles):
            sl = wld[:, (r % 2) * C:((r % 2) + 1) * C]
            tld = p.dma(lambda h, r=r, sl=sl: h.dma_start(sl, I[iname][r * 128:(r + 1) * 128, :]),
                        wait=[toks_sum[r]] + ([tok_rd[r - 2]] if r >= 2 else [tnthr]))
            qle = self.q_le[:, 0:C]
            tle = p.v(lambda h, sl=sl, qle=qle: h.tensor_single_scalar(qle, sl, A["thrq"][:, 1:2], ALU.is_le),
                      wait=[tld, tnthr])
            dtile = dst_of(r)
            tq = p.v(lambda h, sl=sl, dtile=dtile, qle=qle: h.scalar_tensor_tensor(
                dtile, sl, A["thrq"][:, 0:1], qle, ALU.is_ge, ALU.subtract))
            tok_rd[r] = tq
            toks_q[r] = tq
            if spill is not None:
                toks_q[r] = p.dma(lambda h, r=r, dtile=dtile, spill=spill: h.dma_start(spill(r), dtile),
                                  wait=[tq])
        return toks_q, tmw

    def _mm_acc(self, ps, pairs, first_wait):
        p = self.p
        n = len(pairs)
        last = None
        for i, (lq, rq) in enumerate(pairs):
            if i < n - 1:
                p.t_(lambda h, lq=lq, rq=rq, i=i: h.matmul(ps[:], lq, rq, start=(i == 0), stop=False),
                     wait=first_wait if i == 0 else [])
            else:
                last = p.t(lambda h, lq=lq, rq=rq, i=i: h.matmul(ps[:], lq, rq, start=(i == 0), stop=True),
                           wait=first_wait if i == 0 else [])
        return last

    def emit_l0ffn(self, src, dst, first):
        p, A, I = self.p, self.A, self.I
        self.arb.reset(); self.ar32.reset()
        arb, ar32 = self.arb, self.ar32
        wgb = arb.take(ND * INNER)
        wub = arb.take(ND * INNER)
        xqb = arb.take(D)
        hqb = arb.take(INNER)
        xw = ar32.take(2 * D)
        xln32 = ar32.take(D)
        hfull = ar32.take(INNER)
        gbuf = ar32.take(512)
        xnew = ar32.take(D)
        self.q_wld = ar32.t[:, 0:2 * INNER]
        self.q_le = hqb
        if "onecf" not in A:
            self.sb("onecf", (128, 1)); self.sb("onesrf", (1, 128))
            self.sb("thrq", (128, 2)); self.sb("qs", (128, 34))
            self.sb("mwg", (128, 1)); self.sb("mwu", (128, 1)); self.sb("mwd", (128, 1))
            self.sb("esd05", (128, NT)); self.sb("qsc", (128, 4 * 4)); self.sb("qsc2", (128, 2 * 4))
        prev = list(self.prev_done)
        if not hasattr(self, "tok_onecf"):
            self.tok_onecf = p.dma(lambda h: h.dma_start(A["onecf"][:], I["ones_col"]))
            self.tok_onesrf = p.dma(lambda h: h.dma_start(A["onesrf"][:1, :],
                                                          I["ones_col"].rearrange("p one -> one p")))
        self.tok_psB0_q = prev[-1] if prev else None
        self.tok_psB1_q = prev[-1] if prev else None
        mwscr = self.mw_scr

        if first:
            tg, tokmwg = self.quant_matrix("WgT", ND, INNER, lambda r: wgb[:, r * INNER:(r + 1) * INNER],
                                           lambda r: self.tern_gu[r * 128:(r + 1) * 128, :],
                                           A["mwg"][:, 0:1], D * INNER, prev)
            tu, tokmwu = self.quant_matrix("WuT", ND, INNER, lambda r: wub[:, r * INNER:(r + 1) * INNER],
                                           lambda r: self.tern_gu[D + r * 128:D + (r + 1) * 128, :],
                                           A["mwu"][:, 0:1], D * INNER, prev)
            sp1 = p.dma(lambda h: h.dma_start(mwscr[0:1, :].rearrange("one p -> p one"), A["mwg"][:, 0:1]), wait=[tokmwg])
            sp2 = p.dma(lambda h: h.dma_start(mwscr[1:2, :].rearrange("one p -> p one"), A["mwu"][:, 0:1]), wait=[tokmwu])
            tok_w = tg[-1]
            tok_wu = tu[-1]
            tok_w_all = [tg[-1], tu[-1], sp1, sp2]
        else:
            tg = [p.gdma(lambda h, r=r: h.dma_start(wgb[:, r * INNER:(r + 1) * INNER],
                                                    self.tern_gu[r * 128:(r + 1) * 128, :]), wait=prev)
                  for r in range(ND)]
            tu = [p.gdma(lambda h, r=r: h.dma_start(wub[:, r * INNER:(r + 1) * INNER],
                                                    self.tern_gu[D + r * 128:D + (r + 1) * 128, :]), wait=prev)
                  for r in range(ND)]
            tm1 = p.dma(lambda h: h.dma_start(A["mwg"][:, 0:1], mwscr[0:1, :].rearrange("one p -> p one")), wait=prev)
            tm2 = p.dma(lambda h: h.dma_start(A["mwu"][:, 0:1], mwscr[1:2, :].rearrange("one p -> p one")), wait=prev)
            tm3 = p.dma(lambda h: h.dma_start(A["mwd"][:, 0:1], mwscr[2:3, :].rearrange("one p -> p one")), wait=prev)
            tok_w = tg[-1]
            tok_wu = tu[-1]
            tok_w_all = [tg[-1], tu[-1], tm1, tm2, tm3]

        psA = [A["psA0"], A["psA1"], A["psA2"], A["psA3"]]
        psB = [A["psB0"], A["psB1"]]
        tok_psA = [self.tok_psB1_q] * 4
        tok_psB = [self.tok_psB0_q, self.tok_psB1_q]
        tok_xload = [None] * NT
        tok_xfree = [None] * NT
        tok_hq_dma = [None] * NT
        tok_tp = self.tok_psB1_q
        tok_xq_used = None
        tok_hq_used = None

        for c in range(NT):
            u0 = 4 * (c % 4)
            v0 = 2 * (c % 4)
            xsl = xw[:, (c % 2) * D:((c % 2) + 1) * D]
            tok_xload[c] = p.dma(lambda h, c=c, xsl=xsl: h.dma_start(xsl, src[c * 128:(c + 1) * 128, :]),
                                 wait=([tok_xfree[c - 2]] if c >= 2 else (tok_w_all if first else [])))
            tstat = self.ln_stats(c, xsl, [tok_xload[c]])
            tln = p.v(lambda h, xsl=xsl, c=c: h.tensor_scalar(
                xln32, xsl, A["mu"][:, c:c + 1], A["rstd"][:, c:c + 1], ALU.subtract, ALU.mult),
                wait=[tstat, tok_xq_used])
            tok_xfree[c] = tln
            tmx = p.v(lambda h, u0=u0: h.reduce_max(A["qsc"][:, u0:u0 + 1], xln32, AX.X,
                                                    apply_absolute_value=True))
            tcl = p.g(lambda h, u0=u0: h.tensor_single_scalar(A["qsc"][:, u0 + 1:u0 + 2],
                                                              A["qsc"][:, u0:u0 + 1], 1e-5, ALU.max), wait=[tmx])
            trc = p.v(lambda h, u0=u0: h.reciprocal(A["qsc"][:, u0 + 2:u0 + 3], A["qsc"][:, u0 + 1:u0 + 2]),
                      wait=[tcl])
            tsx = p.g(lambda h, u0=u0: h.tensor_single_scalar(A["qsc"][:, u0 + 3:u0 + 4],
                                                              A["qsc"][:, u0 + 2:u0 + 3], 127.0, ALU.mult),
                      wait=[trc])
            tesg = p.g(lambda h, u0=u0, v0=v0: h.tensor_scalar(A["qsc2"][:, v0:v0 + 1], A["qsc"][:, u0 + 1:u0 + 2],
                                                               A["mwg"][:, 0:1], 1.0 / 127, ALU.mult, ALU.mult),
                       wait=[tok_w_all[-1]])
            tesu = p.g(lambda h, u0=u0, v0=v0: h.tensor_scalar(A["qsc2"][:, v0 + 1:v0 + 2], A["qsc"][:, u0 + 1:u0 + 2],
                                                               A["mwu"][:, 0:1], 1.0 / 127, ALU.mult, ALU.mult))
            tq1 = p.v(lambda h, u0=u0: h.tensor_scalar(xln32, xln32, A["qsc"][:, u0 + 3:u0 + 4],
                                                       A["ceff"][:, :1], ALU.mult, ALU.add),
                      wait=[tsx, self.tok_ceff])
            tq2 = p.v(lambda h: h.tensor_single_scalar(xqb, xln32, A["ceff"][:, :1], ALU.subtract),
                      wait=[tok_xq_used])
            tps = None
            for j in range(ND):
                w = [tq2, self.tok_ident, tok_tp] if j == 0 else []
                tps = p.t(lambda h, j=j: h.transpose(A["ps_tp"][:, j * 128:(j + 1) * 128],
                                                     xqb[:, j * 128:(j + 1) * 128], A["ident"][:]), wait=w)
            dst3 = xqb.rearrange("p (j n) -> p j n", j=ND)
            src3 = A["ps_tp"][:].rearrange("p (j n) -> p j n", j=ND)
            tev = p.v(lambda h, dst3=dst3, src3=src3: h.tensor_copy(dst3, src3), wait=[tps])
            tok_tp = tev
            tok_xq_used = None
            last_hq = None
            for q in range(8):
                bg, bu = q % 2, 2 + (q % 2)
                gpairs = [(xqb[:, k * 128:(k + 1) * 128],
                           wgb[:, k * INNER + q * 512: k * INNER + (q + 1) * 512]) for k in range(ND)]
                upairs = [(xqb[:, k * 128:(k + 1) * 128],
                           wub[:, k * INNER + q * 512: k * INNER + (q + 1) * 512]) for k in range(ND)]
                mmg = self._mm_acc(psA[bg], gpairs, [tev, tok_w, tok_psA[bg]])
                mmu = self._mm_acc(psA[bu], upairs, [tev, tok_wu, tok_psA[bu]])
                tok_xq_used = mmu
                tsi = p.s(lambda h, bg=bg, v0=v0: h.activation(gbuf, psA[bg][:], AF.Silu,
                                                               scale=A["qsc2"][:, v0:v0 + 1]),
                          wait=[mmg, tesg])
                tok_psA[bg] = tsi
                th = p.v(lambda h, bu=bu, q=q, v0=v0: h.scalar_tensor_tensor(
                    hfull[:, q * 512:(q + 1) * 512], psA[bu][:], A["qsc2"][:, v0 + 1:v0 + 2], gbuf,
                    ALU.mult, ALU.mult), wait=[mmu, tsi, tesu])
                tok_psA[bu] = th
                last_hq = th
            tmh = p.v(lambda h, u0=u0: h.reduce_max(A["qsc"][:, u0:u0 + 1], hfull, AX.X,
                                                    apply_absolute_value=True), wait=[last_hq])
            tch = p.g(lambda h, u0=u0: h.tensor_single_scalar(A["qsc"][:, u0 + 1:u0 + 2],
                                                              A["qsc"][:, u0:u0 + 1], 1e-5, ALU.max), wait=[tmh])
            trh = p.v(lambda h, u0=u0: h.reciprocal(A["qsc"][:, u0 + 2:u0 + 3], A["qsc"][:, u0 + 1:u0 + 2]),
                      wait=[tch])
            tsh = p.g(lambda h, u0=u0: h.tensor_single_scalar(A["qsc"][:, u0 + 3:u0 + 4],
                                                              A["qsc"][:, u0 + 2:u0 + 3], 127.0, ALU.mult),
                      wait=[trh])
            if first:
                tesd = p.g(lambda h, u0=u0, c=c: h.tensor_single_scalar(
                    A["esd05"][:, c:c + 1], A["qsc"][:, u0 + 1:u0 + 2], 0.5 / 127, ALU.mult))
            else:
                tesd = p.g(lambda h, u0=u0, c=c: h.tensor_scalar(A["esd05"][:, c:c + 1], A["qsc"][:, u0 + 1:u0 + 2],
                                                                 A["mwd"][:, 0:1], 0.5 / 127, ALU.mult, ALU.mult),
                           wait=[tok_w_all[-1]])
            th1 = p.v(lambda h, u0=u0: h.tensor_scalar(hfull, hfull, A["qsc"][:, u0 + 3:u0 + 4],
                                                       A["ceff"][:, :1], ALU.mult, ALU.add), wait=[tsh])
            th2 = p.v(lambda h: h.tensor_single_scalar(hqb, hfull, A["ceff"][:, :1], ALU.subtract),
                      wait=[tok_hq_used])
            ev = None
            for r in range(4):
                tpr = None
                for jj in range(8):
                    j = r * 8 + jj
                    w = [th2, tok_tp] if jj == 0 else []
                    tpr = p.t(lambda h, jj=jj, j=j: h.transpose(A["ps_tp"][:, jj * 128:(jj + 1) * 128],
                                                                hqb[:, j * 128:(j + 1) * 128], A["ident"][:]),
                              wait=w)
                d3 = hqb.rearrange("p (r j n) -> p r j n", r=4, j=8)[:, r]
                s3 = A["ps_tp"][:].rearrange("p (j n) -> p j n", j=8)
                ev = p.v(lambda h, d3=d3, s3=s3: h.tensor_copy(d3, s3), wait=[tpr])
                tok_tp = ev
            tok_hq_used = ev
            tok_hq_dma[c] = p.dma(lambda h, c=c: h.dma_start(self.hq_scr[c], hqb), wait=[ev])


        if first:
            td, tokmwd = self.quant_matrix("WdT", 4 * ND, D, lambda r: wgb[:, (r % 16) * 2048 + (r // 16) * 1024:0] if False else self.wdb_tile(wgb, wub, r),
                                           lambda r: self.tern_d[r * 128:(r + 1) * 128, :],
                                           A["mwd"][:, 0:1], D * INNER, [tok_hq_dma[NT - 1], tok_xq_used])
            sp3 = p.dma(lambda h: h.dma_start(mwscr[2:3, :].rearrange("one p -> p one"), A["mwd"][:, 0:1]), wait=[tokmwd])
            tok_wd = [td[-1], sp3]
            tre = None
            for c in range(NT):
                tre = p.g(lambda h, c=c: h.tensor_scalar(A["esd05"][:, c:c + 1], A["esd05"][:, c:c + 1],
                                                         A["mwd"][:, 0:1], 1.0, ALU.mult, ALU.mult),
                          wait=[tokmwd] if c == 0 else [])
            tok_wd.append(tre)
        else:
            td = [p.gdma(lambda h, r=r: h.dma_start(self.wdb_tile(wgb, wub, r),
                                                    self.tern_d[r * 128:(r + 1) * 128, :]),
                         wait=[tok_hq_dma[NT - 1], tok_xq_used]) for r in range(4 * ND)]
            tok_wd = [td[-1]]

        tok_xload2 = [None] * NT
        tok_xn_dma = [None] * NT
        tok_hql = [None] * NT
        tok_hq_used2 = None
        for c in range(NT):
            xsl = xw[:, (c % 2) * D:((c % 2) + 1) * D]
            tl = p.dma(lambda h, c=c, xsl=xsl: h.dma_start(xsl, src[c * 128:(c + 1) * 128, :]),
                       wait=([tok_xfree[14 + (c % 2)]] + tok_wd if c < 2 else [tok_xload2[c - 2]]))
            tok_hql[c] = p.gdma(lambda h, c=c: h.dma_start(hqb, self.hq_scr[c]),
                                wait=[tok_hq_dma[c], tok_hq_used2] + (tok_wd if c == 0 else []))
            mm_o = [None, None]
            for n in range(2):
                pairs = [(hqb[:, k * 128:(k + 1) * 128],
                          self.wdb_tile(wgb, wub, k)[:, n * 512:(n + 1) * 512]) for k in range(4 * ND)]
                mm_o[n] = self._mm_acc(psB[n], pairs, [tok_hql[c], tok_psB[n]] + (tok_wd if c == 0 else []))
            tok_hq_used2 = mm_o[1]
            evs = [None, None]
            for n in range(2):
                evs[n] = p.v(lambda h, n=n, xsl=xsl, c=c: h.scalar_tensor_tensor(
                    xnew[:, n * 512:(n + 1) * 512], psB[n][:], A["esd05"][:, c:c + 1],
                    xsl[:, n * 512:(n + 1) * 512], ALU.mult, ALU.add),
                    wait=[mm_o[n], tl, tok_xn_dma[c - 1] if c >= 1 else None] + (tok_wd if c < 2 else []))
                tok_psB[n] = evs[n]
            tok_xload2[c] = evs[1]
            tok_xn_dma[c] = p.dma(lambda h, c=c: h.dma_start(dst[c * 128:(c + 1) * 128, :], xnew[:, 0:D]),
                                  wait=[evs[1]])

        self.prev_done = [tok_xn_dma[c] for c in range(NT)] + [tok_hq_used2]
        return [tok_xn_dma[c] for c in range(NT)]

    def wdb_tile(self, wgb, wub, r):
        if r < 16:
            return wgb[:, r * 2048: r * 2048 + 1024] if False else wgb[:, r * 1024:(r + 1) * 1024]
        return wub[:, (r - 16) * 1024:(r - 15) * 1024]



    def ln_inner(self, ybuf, col, wait):
        p, A = self.p, self.A
        ty = A["tiny"]
        t1 = p.v(lambda h, col=col: h.reduce_sum(ty[:, col:col + 1], ybuf, AX.X), wait=wait)
        t2 = p.s(lambda h, col=col: h.activation(A["sq_scrap"][:], ybuf, AF.Square,
                                                 accum_out=ty[:, col + 1:col + 2]), wait=wait)
        t3 = p.g(lambda h, col=col: h.tensor_single_scalar(ty[:, col:col + 1], ty[:, col:col + 1], 1.0 / D, ALU.mult), wait=[t1])
        t4 = p.g(lambda h, col=col: h.tensor_tensor(ty[:, col + 2:col + 3], ty[:, col:col + 1], ty[:, col:col + 1], ALU.mult), wait=[t3])
        t5 = p.g(lambda h, col=col: h.tensor_scalar(ty[:, col + 1:col + 2], ty[:, col + 1:col + 2], 1.0 / D, LN_EPS, ALU.mult, ALU.add), wait=[t2])
        t6 = p.g(lambda h, col=col: h.tensor_tensor(ty[:, col + 1:col + 2], ty[:, col + 1:col + 2], ty[:, col + 2:col + 3], ALU.subtract), wait=[t4, t5])
        t7 = p.v(lambda h, col=col: h.reciprocal(ty[:, col + 2:col + 3], ty[:, col + 1:col + 2]), wait=[t6])
        t8 = p.s(lambda h, col=col: h.activation(ty[:, col + 3:col + 4], ty[:, col + 2:col + 3], AF.Sqrt), wait=[t7])
        return t8

    def cross_max(self, col_ap, out11, scr_bf, wait):
        p, A = self.p, self.A
        t1 = p.v(lambda h: h.tensor_copy(scr_bf, col_ap), wait=wait)
        t2 = p.t(lambda h: h.transpose(A["ps_tp"][0:1, 0:128], scr_bf, A["ident"][:]),
                 wait=[t1, self.tok_ident, self.tok_tpfree])
        t3 = p.v(lambda h: h.reduce_max(out11, A["ps_tp"][0:1, 0:128], AX.X), wait=[t2])
        self.tok_tpfree = t3
        return t3

    def emit_l0attn(self, src, dst):
        p, A, I = self.p, self.A, self.I
        self.arb.reset(); self.ar32.reset()
        arb, ar32 = self.arb, self.ar32
        NH = 16
        o_xlnT = arb.off; xlnT = arb.take(ND * T)
        o_kfm = arb.off; kfm = arb.take(ND * T)
        o_vt = arb.off; vtokp = arb.take(NT * 1040)
        o_R3 = arb.off; R3 = arb.take(8 * D)
        o_oh = arb.off; ohalf = arb.take(8 * D)
        o_wpb = arb.off; wpb = arb.take(2 * D)
        o_misc = arb.off; misc = arb.take(2 * D)
        f32v = arb.t.bitcast(F32)
        lngr = f32v[:, o_wpb // 2: o_wpb // 2 + D]
        lnbr = f32v[:, o_misc // 2: o_misc // 2 + D]
        xw = ar32.take(2 * D)
        xnew = ar32.take(D)
        rpbh = ar32.take(2 * D)
        sc32 = ar32.take(D)
        ybuf = ar32.take(D)
        xlnc = ar32.take(D)
        prev = list(self.prev_done)
        if "wb24" not in A:
            self.sb("wb24", (128, 24)); self.sb("wbq32", (128, 8))
            self.sb("nbh", (128, 2)); self.sb("mxcol", (128, 2))
            self.sb("mxbf", (128, 1), BF16)
            self.sb("b11", (1, 8))
        if "onecf" not in A:
            self.sb("onecf", (128, 1)); self.sb("onesrf", (1, 128))
        if not hasattr(self, "tok_onecf"):
            self.tok_onecf = p.dma(lambda h: h.dma_start(A["onecf"][:], I["ones_col"]))
        if "b2hi" not in A:
            self.sb("b2hi", (1, D), BF16); self.sb("b2lo", (1, D), BF16)
        if not hasattr(self, "tok_onesrf"):
            self.tok_onesrf = p.dma(lambda h: h.dma_start(A["onesrf"][:1, :],
                                                          I["ones_col"].rearrange("p one -> one p")))
        self.tok_tpfree = prev[-1] if prev else None

        PSA, PSB, PSC, PSD = A["PSA"], A["PSB"], A["PSC"], A["PSD"]

        twb = p.dma(lambda h: h.dma_start(A["wb24"][:], I["wb24_in"]), wait=prev)
        twbq = p.g(lambda h: h.tensor_single_scalar(A["wbq32"][:], A["wb24"][:, 0:8], 1.0 / 32, ALU.mult),
                   wait=[twb])

        tok_xload = [None] * NT
        tok_lnT = [None] * NT
        tok_tp = prev[-1] if prev else None
        tok_lnbuf_free = None
        for c in range(NT):
            xsl = xw[:, (c % 2) * D:((c % 2) + 1) * D]
            tok_xload[c] = p.dma(lambda h, c=c, xsl=xsl: h.dma_start(xsl, src[c * 128:(c + 1) * 128, :]),
                                 wait=[tok_lnT[c - 2] if c >= 2 else (prev[-1] if prev else None)])
            tstat = self.ln_stats(c, xsl, [tok_xload[c]])
            lnsl = ohalf[:, 0:D]
            tok_ln = p.v(lambda h, xsl=xsl, lnsl=lnsl, c=c: h.tensor_scalar(
                lnsl, xsl, A["mu"][:, c:c + 1], A["rstd"][:, c:c + 1], ALU.subtract, ALU.mult),
                wait=[tstat, tok_lnbuf_free])
            tps = None
            for j in range(ND):
                w = [tok_ln, self.tok_ident, tok_tp] if j == 0 else []
                tps = p.t(lambda h, j=j, lnsl=lnsl: h.transpose(
                    A["ps_tp"][:, j * 128:(j + 1) * 128], lnsl[:, j * 128:(j + 1) * 128], A["ident"][:]), wait=w)
            tok_lnbuf_free = tps
            dst3 = xlnT.rearrange("p (j n) -> p j n", j=ND)[:, :, c * 128:(c + 1) * 128]
            src3 = A["ps_tp"][:].rearrange("p (j n) -> p j n", j=ND)
            tok_lnT[c] = p.v(lambda h, dst3=dst3, src3=src3: h.tensor_copy(dst3, src3), wait=[tps])
            tok_tp = tok_lnT[c]
        tok_phase1 = tok_lnT[NT - 1]
        self.tok_tpfree = tok_phase1

        tok_kdone = None
        tok_wld = [None, None]
        tok_ps_half = [tok_phase1, tok_phase1]
        tkmx = None
        for e in range(ND):
            wsl = wpb[:, (e % 2) * D:((e % 2) + 1) * D]
            w3 = wsl.rearrange("p (dk n) -> p dk n", dk=ND)
            s3 = I["WwT"].rearrange("(dk p) q -> dk p q", p=128)[:, :, D + e * 128:D + (e + 1) * 128].rearrange("dk p q -> p dk q")
            tok_wld[e % 2] = p.gdma(lambda h, w3=w3, s3=s3: h.dma_start(w3, s3),
                                    wait=[tok_wld[e % 2], prev[-1] if prev else None])
            for g in range(4):
                ph = PSA[:, (g % 2) * 512:((g % 2) + 1) * 512]
                pairs = [(wsl[:, dk * 128:(dk + 1) * 128],
                          xlnT[:, dk * T + g * 512: dk * T + (g + 1) * 512]) for dk in range(ND)]
                mm = self._mm_acc(ph, pairs, [tok_wld[e % 2], tok_phase1, tok_ps_half[g % 2]])
                tev = p.s(lambda h, ph=ph, e=e, g=g: h.activation(
                    kfm[:, e * T + g * 512:e * T + (g + 1) * 512], ph, AF.Identity,
                    bias=A["wb24"][:, 8 + e:9 + e], scale=1.0), wait=[mm, twb])
                tok_ps_half[g % 2] = tev
        tok_kdone = tok_ps_half
        tsq_last = None
        mmn = [None, None, None, None]
        for e in range(ND):
            tsq = p.s(lambda h, e=e: h.activation(rpbh, kfm[:, e * T:(e + 1) * T], AF.Square),
                      wait=[tok_ps_half[0], tok_ps_half[1], tsq_last])
            for gsl in range(4):
                ps = (PSA if gsl < 2 else PSB)[0:1, (gsl % 2) * 512:((gsl % 2) + 1) * 512]
                mmn[gsl] = p.t(lambda h, ps=ps, gsl=gsl, e=e: h.matmul(
                    ps, A["onecf"][:], rpbh[:, gsl * 512:(gsl + 1) * 512],
                    start=(e == 0), stop=(e == ND - 1)),
                    wait=[tsq, self.tok_onecf] + ([tok_ps_half[0], tok_ps_half[1]] if e == 0 else []))
            tsq_last = mmn[3]
        tr1 = p.v(lambda h: h.reduce_max(A["b11"][:1, 0:1], PSA[0:1, 0:1024], AX.X), wait=[mmn[1], mmn[0]])
        tr2 = p.v(lambda h: h.reduce_max(A["b11"][:1, 5:6], PSB[0:1, 0:1024], AX.X), wait=[mmn[3], mmn[2]])
        tr3 = p.v(lambda h: h.tensor_tensor(A["b11"][:1, 0:1], A["b11"][:1, 0:1], A["b11"][:1, 5:6], ALU.max), wait=[tr1, tr2])
        tok_kmax11 = p.s(lambda h: h.activation(A["b11"][:1, 0:1], A["b11"][:1, 0:1], AF.Sqrt), wait=[tr3])
        tok_knorm_ps = [tr1, tr2]

        tvw = []
        for dk in range(ND):
            tvw.append(p.gdma(lambda h, dk=dk: h.dma_start(R3[:, dk * D:(dk + 1) * D],
                                                           I["WwT"][dk * 128:(dk + 1) * 128, 2 * D:3 * D]),
                              wait=[prev[-1] if prev else None]))
        tok_vb = self.bias_hilo_to(I["Wb_row"][:, 2 * D:3 * D], D, A["b2hi"], A["b2lo"])
        tones = p.g(lambda h: h.memset(
            vtokp.rearrange("p (c hh m) -> p c hh m", c=NT, m=65)[:, :, :, 64:65], 1.0))
        tok_vev = [None] * NT
        tok_psv = [tok_knorm_ps[1], tok_knorm_ps[1]]
        for tc in range(NT):
            for half in range(2):
                ph = PSB[:, half * 512:(half + 1) * 512]
                pairs = [(xlnT[:, dk * T + tc * 128:dk * T + (tc + 1) * 128],
                          R3[:, dk * D + half * 512: dk * D + (half + 1) * 512]) for dk in range(ND)]
                mm = self._mm_acc(ph, pairs, [tvw[-1], tok_phase1, tok_psv[half]])
                p.t_(lambda h, ph=ph, half=half: h.matmul(ph, A["ones_row"][:1, :],
                                                          A["b2hi"][:1, half * 512:(half + 1) * 512],
                                                          start=False, stop=False), wait=[tok_vb, self.tok_onesrow])
                mm2 = p.t(lambda h, ph=ph, half=half: h.matmul(ph, A["ones_row"][:1, :],
                                                               A["b2lo"][:1, half * 512:(half + 1) * 512],
                                                               start=False, stop=True))
                d3 = vtokp.rearrange("p (c hh m) -> p c hh m", c=NT, m=65)[:, tc, half * 8:(half + 1) * 8, 0:64]
                s3 = ph.rearrange("p (hh m) -> p hh m", m=64)
                ev = p.v(lambda h, d3=d3, s3=s3: h.tensor_copy(d3, s3), wait=[mm2])
                tok_psv[half] = ev
                tok_vev[tc] = ev
        tok_vdone = tok_vev[NT - 1]

        tok_proj_prev = None
        tok_sc32_free = None
        tok_Pfree = [None, None]
        tok_psS = tok_psv[0]
        tok_psO = [tok_psv[1], tok_psv[1]]
        tok_q_used = None
        tok_ohalf_used = None
        tok_xn_dma = [None] * NT
        for H in range(2):
            tok_qev = [None] * ND
            for e in range(ND):
                wsl = wpb[:, (e % 2) * D:((e % 2) + 1) * D]
                w3 = wsl.rearrange("p (dk n) -> p dk n", dk=ND)
                s3 = I["WwT"].rearrange("(dk p) q -> dk p q", p=128)[:, :, e * 128:(e + 1) * 128].rearrange("dk p q -> p dk q")
                tok_wld[e % 2] = p.gdma(lambda h, w3=w3, s3=s3: h.dma_start(w3, s3),
                                        wait=[tok_wld[e % 2], tok_qev[e - 2] if e >= 2 else (tok_Pfree[0] or tok_vdone), tok_proj_prev])
                for g in range(2):
                    ph = PSA[:, (g % 2) * 512:((g % 2) + 1) * 512]
                    pairs = [(wsl[:, dk * 128:(dk + 1) * 128],
                              xlnT[:, dk * T + H * 1024 + g * 512: dk * T + H * 1024 + (g + 1) * 512])
                             for dk in range(ND)]
                    mm = self._mm_acc(ph, pairs, [tok_wld[e % 2], tok_psS, tok_q_used])
                    tev = p.s(lambda h, ph=ph, e=e, g=g: h.activation(
                        R3[:, e * D + g * 512:e * D + (g + 1) * 512], ph, AF.Identity,
                        bias=A["wbq32"][:, e:e + 1], scale=1.0 / 32), wait=[mm, twbq])
                    tok_psS = tev
                tok_qev[e] = tok_psS
            tsq_last = None
            mmn = [None, None]
            for e in range(ND):
                tsq = p.s(lambda h, e=e: h.activation(rpbh[:, 0:D], R3[:, e * D:(e + 1) * D], AF.Square),
                          wait=[tok_qev[e], tsq_last])
                for gsl in range(2):
                    ps = PSA[0:1, gsl * 512:(gsl + 1) * 512]
                    mmn[gsl] = p.t(lambda h, ps=ps, gsl=gsl, e=e: h.matmul(
                        ps, A["onecf"][:], rpbh[:, gsl * 512:(gsl + 1) * 512],
                        start=(e == 0), stop=(e == ND - 1)),
                        wait=[tsq, self.tok_onecf] + ([tok_psS] if e == 0 else []))
                tsq_last = mmn[1]
            tq1 = p.v(lambda h: h.reduce_max(A["b11"][:1, 1:2], PSA[0:1, 0:1024], AX.X), wait=[mmn[0], mmn[1]])
            tok_psS = tq1
            tok_qmax11 = p.s(lambda h: h.activation(A["b11"][:1, 1:2], A["b11"][:1, 1:2], AF.Sqrt), wait=[tq1])
            tqk = p.g(lambda h: h.tensor_tensor(A["b11"][:1, 2:3], A["b11"][:1, 0:1], A["b11"][:1, 1:2], ALU.mult),
                      wait=[tok_qmax11, tok_kmax11])
            tqk = p.g(lambda h: h.tensor_scalar(A["b11"][:1, 2:3], A["b11"][:1, 2:3], 1.0, 2.0, ALU.mult, ALU.add))

            for h_ in range(NH):
                hp = h_ % 2
                psO = PSB[0:65, :] if hp == 0 else PSC[0:65, :]
                trp = p.dma(lambda hh, h_=h_: hh.dma_start(
                    rpbh.rearrange("p (d n) -> p d n", d=16),
                    I["rpb_tiles"][h_].rearrange("d p n -> p d n")),
                    wait=[tok_sc32_free, tok_proj_prev, tok_qmax11])
                trm = p.v(lambda hh: hh.reduce_max(A["mxcol"][:, 1:2], rpbh, AX.X), wait=[trp])
                trm11 = self.cross_max(A["mxcol"][:, 1:2], A["b11"][:1, 3:4], A["mxbf"][:, 0:1], [trm])
                tb = p.g(lambda hh: hh.tensor_tensor(A["b11"][:1, 4:5], A["b11"][:1, 2:3], A["b11"][:1, 3:4], ALU.add),
                         wait=[trm11, tqk])
                tb = p.g(lambda hh: hh.tensor_single_scalar(A["b11"][:1, 4:5], A["b11"][:1, 4:5], -1.0, ALU.mult))
                tbc = p.t(lambda hh: hh.matmul(PSD[:, 0:1], A["onesrf"][:1, :], A["b11"][:1, 4:5],
                                               start=True, stop=True), wait=[tb, self.tok_onesrf, getattr(self, "tok_psd_free", None)])
                tnb = p.v(lambda hh, hp=hp: hh.tensor_copy(A["nbh"][:, hp:hp + 1], PSD[:, 0:1]), wait=[tbc])
                self.tok_psd_free = tnb
                ktile = kfm
                ex_tok = None
                pv_last = None
                kc_hi = 8 * H + 8
                for kc in range(kc_hi):
                    qs0 = max(0, (kc - 8 * H)) * 128
                    pieces = []
                    st = qs0
                    if st < 512:
                        pieces.append((st, 512 - st)); st = 512
                    pieces.append((st, 1024 - st)) if st < 1024 else None
                    lhs_k = kfm[64 * (h_ % 2):64 * (h_ % 2) + 64, (h_ // 2) * T + kc * 128:(h_ // 2) * T + (kc + 1) * 128]
                    sc_toks = []
                    for (st, ln) in pieces:
                        ph = PSA[:, st:st + ln]
                        rhs_q = R3[64 * (h_ % 2):64 * (h_ % 2) + 64, (h_ // 2) * D + st:(h_ // 2) * D + st + ln]
                        sc_toks.append(p.t(lambda hh, ph=ph, lhs_k=lhs_k, rhs_q=rhs_q: hh.matmul(
                            ph, lhs_k, rhs_q, start=True, stop=True),
                            wait=[tok_psS, tok_qev[ND - 1], tok_kdone[0], tok_kdone[1]]))
                    N = 1024 - qs0
                    d0 = (8 * H - kc) * 128 + qs0
                    tadd = p.v(lambda hh, qs0=qs0, N=N, d0=d0: hh.tensor_tensor(
                        sc32[:, qs0:qs0 + N], PSA[:, qs0:qs0 + N], rpbh[:, d0:d0 + N], ALU.add),
                        wait=sc_toks + [trp, tok_sc32_free])
                    tok_psS = tadd
                    Pb = wpb[:, hp * D: hp * D + D]
                    tex = p.s(lambda hh, qs0=qs0, N=N, Pb=Pb, hp=hp: hh.activation(
                        Pb[:, qs0:qs0 + N], sc32[:, qs0:qs0 + N], AF.Exp, bias=A["nbh"][:, hp:hp + 1]),
                        wait=[tadd, tnb, tok_Pfree[hp]])
                    tok_sc32_free = tex
                    lhs_v = vtokp[:, kc * 1040 + h_ * 65: kc * 1040 + (h_ + 1) * 65]
                    for (st, ln) in pieces:
                        po = psO[:, st:st + ln]
                        pv_last = p.t(lambda hh, po=po, lhs_v=lhs_v, Pb=Pb, st=st, ln=ln, kc=kc: hh.matmul(
                            po, lhs_v, Pb[:, st:st + ln], start=(kc == 0), stop=(kc == kc_hi - 1)),
                            wait=[tex, tok_vdone, tok_psO[hp]] if kc == 0 else [tex])
                tok_Pfree[hp] = pv_last
                trd = p.v(lambda hh, psO=psO: hh.reciprocal(xlnc[0:1, 0:1024], psO[64:65, :]), wait=[pv_last])
                bt = [None, None]
                for bh in range(2):
                    tb1 = p.t(lambda hh, bh=bh: hh.matmul(PSD[0:64, :], A["onesrf"][:1, 0:64],
                                                          xlnc[0:1, bh * 512:(bh + 1) * 512], start=True, stop=True),
                              wait=[trd, self.tok_psd_free])
                    tb2 = p.v(lambda hh, bh=bh: hh.tensor_copy(ybuf[0:64, bh * 512:(bh + 1) * 512], PSD[0:64, :]),
                              wait=[tb1])
                    self.tok_psd_free = tb2
                    bt[bh] = tb2
                ev = [None, None]
                for bh in range(2):
                    osl = ohalf[64 * (h_ % 2):64 * (h_ % 2) + 64, (h_ // 2) * D + bh * 512:(h_ // 2) * D + (bh + 1) * 512]
                    ev[bh] = p.v(lambda hh, osl=osl, psO=psO, bh=bh: hh.tensor_mul(
                        osl, psO[0:64, bh * 512:(bh + 1) * 512], ybuf[0:64, bh * 512:(bh + 1) * 512]),
                        wait=[bt[bh], tok_ohalf_used])
                tok_psO[hp] = ev[1]
            tok_heads_done = ev[1]
            tok_q_used = sc_toks[-1]

            tpj = []
            for j in range(ND):
                tpj.append(p.gdma(lambda h, j=j: h.dma_start(R3[:, j * D:(j + 1) * D],
                                                             I["projT"][j * 128:(j + 1) * 128, :]),
                                  wait=[tok_q_used, tok_wld[j % 2] if j < 2 else None]))
            tok_pb = self.bias_hilo_to(I["pjb_row"], D, A["b2hi"], A["b2lo"], extra_wait=[tok_vdone])
            tlg0 = p.dma(lambda h: h.dma_start(sc32, I["lng_rep"]), wait=[tok_sc32_free, tok_Pfree[0], tok_Pfree[1]])
            tlg1 = p.v(lambda h: h.tensor_copy(lngr, sc32), wait=[tlg0])
            tlb0 = p.dma(lambda h: h.dma_start(sc32, I["lnb_rep"]), wait=[tlg1])
            tlb1 = p.v(lambda h: h.tensor_copy(lnbr, sc32), wait=[tlb0])
            tok_lnw = tlb1
            for cl in range(8):
                gc = 8 * H + cl
                xsl = xw[:, (gc % 2) * D:((gc % 2) + 1) * D]
                tx = p.dma(lambda h, gc=gc, xsl=xsl: h.dma_start(xsl, src[gc * 128:(gc + 1) * 128, :]),
                           wait=[tok_xn_dma[gc - 2] if gc >= 2 else tok_heads_done,
                                 tok_lnT[14 + (gc % 2)] if H == 0 and cl < 2 else None])
                txl = p.v(lambda h, xsl=xsl, gc=gc: h.tensor_scalar(
                    xlnc, xsl, A["mu"][:, gc:gc + 1], A["rstd"][:, gc:gc + 1], ALU.subtract, ALU.mult),
                    wait=[tx, tok_heads_done])
                mmp = [None, None]
                for bh in range(2):
                    ph = PSA[:, bh * 512:(bh + 1) * 512]
                    pairs = [(ohalf[:, j * D + cl * 128:j * D + cl * 128 + 128],
                              R3[:, j * D + bh * 512:j * D + (bh + 1) * 512]) for j in range(ND)]
                    mm = self._mm_acc(ph, pairs, [tpj[-1], tok_heads_done, tok_psS if cl == 0 else mm_free[bh]])
                    p.t_(lambda h, ph=ph, bh=bh: h.matmul(ph, A["ones_row"][:1, :],
                                                          A["b2hi"][:1, bh * 512:(bh + 1) * 512], start=False, stop=False),
                         wait=[tok_pb, self.tok_onesrow])
                    mmp[bh] = p.t(lambda h, ph=ph, bh=bh: h.matmul(ph, A["ones_row"][:1, :],
                                                                   A["b2lo"][:1, bh * 512:(bh + 1) * 512], start=False, stop=True))
                ty = [None, None]
                for bh in range(2):
                    ty[bh] = p.v(lambda h, bh=bh: h.tensor_add(ybuf[:, bh * 512:(bh + 1) * 512],
                                                               PSA[:, bh * 512:(bh + 1) * 512], xlnc[:, bh * 512:(bh + 1) * 512]),
                                 wait=[mmp[bh], txl])
                mm_free = ty
                col = 20 + 4 * (cl % 2)
                tst = self.ln_inner(ybuf, col, [ty[1]])
                tzn = p.v(lambda h, col=col: h.tensor_scalar(ybuf, ybuf, A["tiny"][:, col:col + 1],
                                                             A["tiny"][:, col + 3:col + 4], ALU.subtract, ALU.mult),
                          wait=[tst])
                tg1 = p.v(lambda h: h.tensor_mul(sc32, ybuf, lngr), wait=[tok_lnw, tlb0])
                tg2 = p.v(lambda h: h.tensor_add(sc32, sc32, lnbr))
                txn = p.v(lambda h, xsl=xsl: h.tensor_add(xnew, sc32, xsl),
                          wait=[tok_xn_dma[gc - 1] if gc >= 1 else None])
                tok_sc32_free = txn
                tok_xn_dma[gc] = p.dma(lambda h, gc=gc: h.dma_start(dst[gc * 128:(gc + 1) * 128, :], xnew),
                                       wait=[txn])
            tok_proj_prev = tok_xn_dma[8 * H + 7]
            tok_ohalf_used = mmp[1]
            tok_Pfree = [tok_lnw, tok_lnw]

        self.prev_done = [tok_xn_dma[c] for c in range(NT)] + [tok_sc32_free]
        return [tok_xn_dma[c] for c in range(NT)]


def build_rpb_tiles(rpb):
    S = T
    out = np.empty((16, 16, 128, 128), np.float32)
    qk = np.arange(128)
    for d in range(16):
        idx = (qk[:, None] - qk[None, :] - 128 * d) + (S - 1)
        idx = np.clip(idx, 0, 2 * S - 2)
        for h in range(16):
            out[h, d] = rpb[idx, h]
        if d == 0:
            mask = qk[:, None] > qk[None, :]
            for h in range(16):
                out[h, 0][mask] = -1e30
    return out


def host_inputs(raw, sublayers):
    m = {}
    need = set(sublayers)
    f32 = lambda a: np.ascontiguousarray(np.asarray(a), dtype=np.float32)
    row = lambda v: f32(v).reshape(1, -1)
    rep = lambda v: np.ascontiguousarray(np.broadcast_to(f32(v), (128, len(np.asarray(v)))))
    if need & {"S4", "S6"}:
        m["w1"] = f32(raw["L1_ffn_w1"]); m["w2"] = f32(raw["L1_ffn_w2"])
        m["b1_row"] = row(raw["L1_ffn_b1"]); m["b2_row"] = row(raw["L1_ffn_b2"])
    if need & {"S1", "S3"}:
        m["WgT"] = f32(np.asarray(raw["L0_ffn_w_gate"]).T)
        m["WuT"] = f32(np.asarray(raw["L0_ffn_w_up"]).T)
        m["WdT"] = f32(np.asarray(raw["L0_ffn_w_down"]).T)
    if "S2" in need:
        m["WwT"] = f32(np.asarray(raw["L0_attn_W_w"]).T)
        m["projT"] = f32(np.asarray(raw["L0_attn_proj_w"]).T)
        m["Wb_row"] = row(raw["L0_attn_W_b"]); m["pjb_row"] = row(raw["L0_attn_proj_b"])
        m["wb24_in"] = np.ascontiguousarray(f32(raw["L0_attn_W_b"]).reshape(24, 128).T)
        m["lng_rep"] = rep(raw["L0_attn_ln_g"]); m["lnb_rep"] = rep(raw["L0_attn_ln_b"])
        m["rpb_tiles"] = build_rpb_tiles(np.asarray(raw["L0_attn_rpb"]))
    if "S5" in need:
        m["WkT"] = f32(np.asarray(raw["L1_attn_Wk"]).T)
        m["WvT"] = f32(np.asarray(raw["L1_attn_Wv"]).T)
        m["WrT"] = f32(np.asarray(raw["L1_attn_Wr"]).T)
        m["WoT"] = f32(np.asarray(raw["L1_attn_Wo"]).T)
        m["bo_row"] = row(raw["L1_attn_bo"])
        c8 = lambda v: np.ascontiguousarray(f32(v).reshape(ND, 128).T)
        m["u_in"] = c8(raw["L1_attn_u"]); m["w_in"] = c8(raw["L1_attn_w"])
        m["tmk_in"] = c8(raw["L1_attn_tmk"]); m["tmv_in"] = c8(raw["L1_attn_tmv"])
        m["tmr_in"] = c8(raw["L1_attn_tmr"])
    m["ones_col"] = np.ones((128, 1), np.float32)
    m["ident_bf"] = np.eye(128, dtype=np.float32).astype(ml_dtypes.bfloat16)
    m["ones_row_bf"] = np.ones((1, 128), np.float32).astype(ml_dtypes.bfloat16)
    return m


def build(sublayers):
    mk = MK(sublayers)
    mk.prologue()
    cur_src = mk.I["x0"]
    toks = None
    first = {"l0": True}
    for i, s in enumerate(sublayers):
        dst = mk.out_ap if i == len(sublayers) - 1 else mk.xres
        if s in ("S4", "S6"):
            toks = mk.emit_l1ffn(cur_src, dst)
        elif s == "S5":
            toks = mk.emit_l1attn(cur_src, dst)
        elif s in ("S1", "S3"):
            toks = mk.emit_l0ffn(cur_src, dst, first.get("l0", True)); first["l0"] = False
        elif s == "S2":
            toks = mk.emit_l0attn(cur_src, dst)
        else:
            raise NotImplementedError(s)
        cur_src = mk.xres
    mk.p.finalize(toks)
    return mk


def run(sublayers, x_full, raw, n_cores=8):
    mk = build(sublayers)
    tmpl = host_inputs(raw, sublayers)
    in_maps = []
    for i in range(n_cores):
        m = dict(tmpl)
        m["x0"] = np.ascontiguousarray(np.asarray(x_full[i]), dtype=np.float32)
        in_maps.append(m)
    res = run_bass_kernel_spmd(mk.nc, in_maps, core_ids=list(range(n_cores)))
    return np.stack([r["out"] for r in res.results])


SUBLAYERS = ("S1", "S2", "S3", "S4", "S5", "S6")


def kernel(**inputs):
    raw = inputs
    x_full = np.ascontiguousarray(np.asarray(raw["L0_attn_x"]), dtype=np.float32)
    n_cores = 8
    mk = build(SUBLAYERS)
    tmpl = host_inputs(raw, SUBLAYERS)
    in_maps = []
    for i in range(n_cores):
        m = dict(tmpl)
        m["x0"] = np.ascontiguousarray(x_full[i])
        in_maps.append(m)
    res = run_bass_kernel_spmd(mk.nc, in_maps, core_ids=list(range(n_cores)))
    return np.stack([np.asarray(r["out"], np.float32) for r in res.results])
